# version 17
# baseline (speedup 1.0000x reference)
import numpy as np
from contextlib import ExitStack

import concourse.bass as bass
import concourse.mybir as mybir
from concourse.alu_op_type import AluOpType as OP
from concourse.bass_utils import run_bass_kernel_spmd

F32 = mybir.dt.float32
BF16 = mybir.dt.bfloat16
AF = mybir.ActivationFunctionType
AX = mybir.AxisListType.X

D = 2048
T = 2048
NT = 16
S = 2
TPS = 8
KC = 16
NJ = 16
JW = 512
NOB = 4
EPS_LN = 1e-5
EPS_AT = 1e-6

ENGINES = ("sync", "scalar", "vector", "tensor", "gpsimd")
DEFSEM = {"sync": "wd", "scalar": "act", "vector": "dve", "tensor": "pe",
          "gpsimd": "gp"}
SEMS = ("wd", "xd", "gd", "pe", "dve", "act", "gp", "cc")


class KB:
    def __init__(self):
        self.ops = {e: [] for e in ENGINES}
        self.cnt = {s: 0 for s in SEMS}
        self.lastw = {e: {s: 0 for s in SEMS} for e in ENGINES}
        self.ready = {}
        self.free = {}
        self.pending = {e: [] for e in ENGINES}
        self.strictset = set()
        self.semh = None

    def _wait(self, eng, sem, val):
        if val <= 0 or self.lastw[eng][sem] >= val:
            return
        assert val <= self.cnt[sem], f"wait {sem}={val} > emitted {self.cnt[sem]}"
        self.lastw[eng][sem] = val

        def f(e, s=sem, v=val):
            e.wait_ge(self.semh[s], v)
        self.ops[eng].append(f)

    def op(self, eng, fn, reads=(), writes=(), sem=None, n=1, inc=True,
           strict=False, sreads=()):
        own = DEFSEM[eng] if sem is None else "__dma__"
        for r in reads:
            if r in self.ready:
                s2, v2 = self.ready[r]
                if s2 != own or r in self.strictset or r in sreads:
                    self._wait(eng, s2, v2)
        for r in sreads:
            if r in self.ready:
                self._wait(eng, *self.ready[r])
        for w in writes:
            for s2, v2 in self.free.get(w, {}).items():
                if s2 != own:
                    self._wait(eng, s2, v2)
            if w in self.ready:
                s2, v2 = self.ready[w]
                if s2 != own:
                    self._wait(eng, s2, v2)
        if not inc:
            self.ops[eng].append(fn)
            self.pending[eng].extend(reads)
            return None
        if sem is None:
            sem = DEFSEM[eng]
        self.cnt[sem] += n
        val = self.cnt[sem]

        def f(e, fn=fn, s=sem, n=n):
            fn(e).then_inc(self.semh[s], n)
        self.ops[eng].append(f)
        for w in writes:
            self.ready[w] = (sem, val)
            self.free[w] = {}
            if strict:
                self.strictset.add(w)
            else:
                self.strictset.discard(w)
        for r in list(reads) + self.pending[eng]:
            self.free.setdefault(r, {})[sem] = val
        self.pending[eng] = []
        return val


def build_program():
    nc = bass.Bass("TRN2", target_bir_lowering=False)
    x_in = nc.dram_tensor("x_in", [T, D], F32, kind="ExternalInput").ap()
    wg = nc.dram_tensor("wg", [NJ, KC, 128, JW], F32, kind="ExternalInput").ap()
    wu = nc.dram_tensor("wu", [NJ, KC, 128, JW], F32, kind="ExternalInput").ap()
    wdn = nc.dram_tensor("wdn", [NJ, 4, 128, D], F32, kind="ExternalInput").ap()
    wq = nc.dram_tensor("wq", [NOB, KC, 128, JW], F32, kind="ExternalInput").ap()
    wo = nc.dram_tensor("wo", [KC, NOB, 128, JW], F32, kind="ExternalInput").ap()
    wk = nc.dram_tensor("wk", [KC, 128, JW], F32, kind="ExternalInput").ap()
    wv = nc.dram_tensor("wv", [KC, 128, JW], F32, kind="ExternalInput").ap()
    bqd = nc.dram_tensor("bq", [1, D], F32, kind="ExternalInput").ap()
    bkd = nc.dram_tensor("bk", [1, JW], F32, kind="ExternalInput").ap()
    bvd = nc.dram_tensor("bv", [1, JW], F32, kind="ExternalInput").ap()
    identd = nc.dram_tensor("ident", [128, 128], F32, kind="ExternalInput").ap()
    outd = nc.dram_tensor("out", [T, D], F32, kind="ExternalOutput").ap()
    park1 = nc.dram_tensor("park1", [T, D], F32).ap()
    park2 = nc.dram_tensor("park2", [T, D], F32).ap()
    cc_in_t = nc.dram_tensor("cc_in", [128, 528], F32)
    cc_out_t = nc.dram_tensor("cc_out", [128, 528], F32)
    wgb = nc.dram_tensor("wgb", [NJ, 2, KC, 128, 512], BF16).ap()
    wub = nc.dram_tensor("wub", [NJ, 2, KC, 128, 512], BF16).ap()
    wdb = nc.dram_tensor("wdb", [NJ, 4, 2, 128, 2048], BF16).ap()
    wqb = nc.dram_tensor("wqb", [NOB, KC, 128, 1024], BF16).ap()
    wob = nc.dram_tensor("wob", [KC, NOB, 128, 1024], BF16).ap()
    wkb = nc.dram_tensor("wkb", [KC, 128, 1024], BF16).ap()
    wvb = nc.dram_tensor("wvb", [KC, 128, 1024], BF16).ap()
    DSTN = {id(park1): "park1", id(park2): "park2", id(outd): "out"}

    kb = KB()
    with ExitStack() as es:
        def sb(name, cols, dt=F32, parts=128):
            return es.enter_context(nc.sbuf_tensor(name, [parts, cols], dt))
        x_cur = sb("x_cur", TPS * D)
        yfh = sb("yfh", KC * 1024, BF16)
        yfl = sb("yfl", KC * 1024, BF16)
        rg = sb("rg", 4096, BF16)
        ru = sb("ru", 4096, BF16)
        rd = sb("rd", 4096, BF16)
        y_tok = sb("y_tok", D)
        scr = sb("scr", 2048)
        hfmb = sb("hfmb", 4096, BF16)
        afh = sb("afh", D, BF16)
        afl = sb("afl", D, BF16)
        skb = sb("skb", 1024, BF16)
        veh = sb("veh", 528, BF16)
        vel = sb("vel", 528, BF16)
        kv_sb = sb("kv_sb", 528)
        kvb = sb("kvb", 1056, BF16)
        sqfb = sb("sqfb", 1024, BF16)
        ln = sb("ln", 64)
        rr = sb("rr", 32)
        id_sb = sb("id_sb", 128)
        ones1 = sb("ones1", 128, parts=1)
        bq_sb = sb("bq_sb", D, parts=1)
        bk_sb = sb("bk_sb", JW, parts=1)
        bv_sb = sb("bv_sb", JW, parts=1)
        P = [es.enter_context(nc.psum_tensor(f"P{i}", [128, 512], F32))
             for i in range(8)]
        semh = {s: es.enter_context(nc.semaphore(s)) for s in SEMS}
        kb.semh = semh

        scr0 = scr[:, 0:512]
        scr1 = scr[:, 512:1024]
        scr2 = scr[:, 1024:1536]
        sp_e = scr[:, 1536:1664]
        sp_f = scr[:, 1664:1792]
        sp_t = scr[:, 1792:1920]
        SCRALL = ["scr0", "scr1", "scr2", "sp_e", "sp_f", "sp_t", "scrx"]
        hfm_h = hfmb[:, 0:2048]
        hfm_l = hfmb[:, 2048:4096]
        sg = y_tok

        XC = [f"xc{i}" for i in range(TPS)]
        YF = [f"yf{i}" for i in range(TPS)]
        PK = [f"P{i}" for i in range(8)]
        SLOT = ["s0", "s1", "s2", "s3", "s4", "s5"]
        slot_ap = {"s0": rg[:, 0:2048], "s1": rg[:, 2048:4096],
                   "s2": ru[:, 0:2048], "s3": ru[:, 2048:4096],
                   "s4": rd[:, 0:2048], "s5": rd[:, 2048:4096]}
        yfh3 = yfh[:, :].rearrange("p (k t) -> p k t", k=KC)
        yfl3 = yfl[:, :].rearrange("p (k t) -> p k t", k=KC)

        for apd, aps, res in ((bqd, bq_sb, "bq_sb"), (bkd, bk_sb, "bk_sb"),
                              (bvd, bv_sb, "bv_sb"), (identd, id_sb, "ident")):
            kb.op("scalar", lambda e, a=aps[:, :], r=apd[:, :]:
                  e.dma_start(out=a, in_=r), writes=[res], sem="xd", n=16)
        kb.op("vector", lambda e: e.memset(ones1[:, :], 1.0), writes=["ones1"])

        def presplit():
            chunks = []
            for j in range(NJ):
                for q in range(8):
                    chunks.append((
                        wg[j, 2 * q:2 * q + 2, :, :], 512,
                        wgb[j, :, 2 * q:2 * q + 2, :, 0:256],
                        wgb[j, :, 2 * q:2 * q + 2, :, 256:512], f"Bwg_{j}_{q}"))
                    chunks.append((
                        wu[j, 2 * q:2 * q + 2, :, :], 512,
                        wub[j, :, 2 * q:2 * q + 2, :, 0:256],
                        wub[j, :, 2 * q:2 * q + 2, :, 256:512], f"Bwu_{j}_{q}"))
                for kd in range(4):
                    for hh in range(2):
                        chunks.append((
                            wdn[j, kd, :, hh * 1024:(hh + 1) * 1024], 1024,
                            wdb[j, kd, hh, :, 0:1024],
                            wdb[j, kd, hh, :, 1024:2048], f"Bwd_{j}_{kd}_{hh}"))
            for q in range(8):
                chunks.append((
                    wk[2 * q:2 * q + 2, :, :], 512,
                    wkb[2 * q:2 * q + 2, :, 0:512],
                    wkb[2 * q:2 * q + 2, :, 512:1024], f"Bwk_{q}"))
                chunks.append((
                    wv[2 * q:2 * q + 2, :, :], 512,
                    wvb[2 * q:2 * q + 2, :, 0:512],
                    wvb[2 * q:2 * q + 2, :, 512:1024], f"Bwv_{q}"))
            for ob in range(NOB):
                for q in range(8):
                    chunks.append((
                        wq[ob, 2 * q:2 * q + 2, :, :], 512,
                        wqb[ob, 2 * q:2 * q + 2, :, 0:512],
                        wqb[ob, 2 * q:2 * q + 2, :, 512:1024], f"Bwq_{ob}_{q}"))
            for k in range(KC):
                for hh in range(2):
                    chunks.append((
                        wo[k, 2 * hh:2 * hh + 2, :, :], 512,
                        wob[k, 2 * hh:2 * hh + 2, :, 0:512],
                        wob[k, 2 * hh:2 * hh + 2, :, 512:1024], f"Bwo_{k}_{hh}"))
            fslots = [(x_cur[:, 12288 + 1024 * i:13312 + 1024 * i],
                       "xc6" if i < 2 else "xc7") for i in range(4)]
            hslots = [(skb[:, :], ["skh", "skl"]), (sqfb[:, :], ["sqh", "sql"])]
            lslots = [(afh[:, 0:1024], ["afh0", "afh1"]),
                      (afh[:, 1024:2048], ["afh2", "afh3"])]
            for i, (srcA, w, hiA, loA, res) in enumerate(chunks):
                nd = len(srcA.shape)
                src3 = srcA.rearrange("k p w -> p k w") if nd == 3 else srcA
                fap, rf = fslots[i % 4]
                hap, rhl = hslots[i % 2]
                lap, rll = lslots[i % 2]
                f3 = fap.rearrange("p (k w) -> p k w", w=w) if nd == 3 else fap
                if len(hiA.shape) == 4:
                    kb.op("sync", lambda e, d=f3, s=src3:
                          e.dma_start(out=d, in_=s),
                          writes=[rf], sem="wd", n=16)
                    kb.op("scalar", lambda e, o=hap, a=fap:
                          e.activation(o, a, AF.Copy),
                          reads=[rf], writes=rhl)
                    kb.op("vector", lambda e, o=lap, a=fap, h=hap:
                          e.tensor_tensor(o, a, h, OP.subtract),
                          reads=[rf] + rhl, writes=rll)
                    hv = hap.rearrange("p (k h w) -> p k h w", k=2, h=2)
                    lv = lap.rearrange("p (k h w) -> p k h w", k=2, h=2)
                    for hx in range(2):
                        dh = hiA[hx, :, :, :].rearrange("k p w -> p k w")
                        dl = loA[hx, :, :, :].rearrange("k p w -> p k w")
                        kb.op("scalar", lambda e, d=dh, s=hv[:, :, hx, :]:
                              e.dma_start(out=d, in_=s),
                              reads=rhl, writes=[res] if hx == 1 else (),
                              sem="xd", n=16)
                        kb.op("scalar", lambda e, d=dl, s=lv[:, :, hx, :]:
                              e.dma_start(out=d, in_=s),
                              reads=rll, writes=[res + "l"] if hx == 1 else (),
                              sem="xd", n=16)
                    continue
                if True:
                    hi3 = hiA.rearrange("k p w -> p k w") if nd == 3 else hiA
                    lo3 = loA.rearrange("k p w -> p k w") if nd == 3 else loA
                    h3 = hap.rearrange("p (k w) -> p k w", w=w) if nd == 3 else hap
                    l3 = lap.rearrange("p (k w) -> p k w", w=w) if nd == 3 else lap
                kb.op("sync", lambda e, d=f3, s=src3: e.dma_start(out=d, in_=s),
                      writes=[rf], sem="wd", n=16)
                kb.op("scalar", lambda e, o=hap, a=fap:
                      e.activation(o, a, AF.Copy),
                      reads=[rf], writes=rhl)
                kb.op("vector", lambda e, o=lap, a=fap, h=hap:
                      e.tensor_tensor(o, a, h, OP.subtract),
                      reads=[rf] + rhl, writes=rll)
                kb.op("scalar", lambda e, d=hi3, s=h3: e.dma_start(out=d, in_=s),
                      reads=rhl, writes=[res], sem="xd", n=16)
                kb.op("scalar", lambda e, d=lo3, s=l3: e.dma_start(out=d, in_=s),
                      reads=rll, writes=[res + "l"], sem="xd", n=16)

        presplit()

        def load_x(src, srcname, s, tt):
            gt = s * TPS + tt
            xa = x_cur[:, tt * D:(tt + 1) * D]
            srcres = [] if srcname is None else [f"{srcname}_{gt}"]
            kb.op("scalar", lambda e, a=xa, r=src[gt * 128:(gt + 1) * 128, :]:
                  e.dma_start(out=a, in_=r),
                  reads=srcres, writes=[XC[tt]], sem="xd", n=16)

        def stage_a_stats(src, srcname, s, tt):
            gt = s * TPS + tt
            xa = x_cur[:, tt * D:(tt + 1) * D]
            ybuf = y_tok[:, :] if tt % 2 == 0 else scr[:, 0:2048]
            yres = ["y_tok"] if tt % 2 == 0 else SCRALL
            c_sum = ln[:, tt:tt + 1]
            c_negm = ln[:, 8 + tt:9 + tt]
            c_ssq = ln[:, 16 + tt:17 + tt]
            c_t0 = ln[:, 24 + tt:25 + tt]
            c_t1 = ln[:, 32 + tt:33 + tt]
            c_rstd = ln[:, 40 + tt:41 + tt]
            kb.op("vector", lambda e, a=xa, o=c_sum: e.reduce_sum(o, a, axis=AX),
                  reads=[XC[tt]], writes=[f"lnsum{tt}"], strict=True)
            kb.op("vector", lambda e, i=c_sum, o=c_negm:
                  e.tensor_scalar(o, i, -1.0 / D, None, OP.mult),
                  reads=[f"lnsum{tt}"], writes=[f"negm{tt}"], strict=True)
            kb.op("scalar", lambda e, a=xa, o=ybuf, b=c_negm, acc=c_ssq:
                  e.activation(o, a, AF.Square, bias=b, scale=1.0, accum_out=acc),
                  reads=[XC[tt], f"negm{tt}"], writes=yres + [f"ssq{tt}"])
            kb.op("vector", lambda e, i=c_ssq, o=c_t0:
                  e.tensor_scalar(o, i, 1.0 / D, EPS_LN, OP.mult, OP.add),
                  reads=[f"ssq{tt}"], writes=[f"lt0{tt}"], strict=True)
            kb.op("vector", lambda e, i=c_t0, o=c_t1: e.reciprocal(o, i),
                  reads=[f"lt0{tt}"], writes=[f"lt1{tt}"], strict=True)
            kb.op("scalar", lambda e, i=c_t1, o=c_rstd:
                  e.activation(o, i, AF.Sqrt), reads=[f"lt1{tt}"], writes=[f"rstd{tt}"])

        def stage_a_norm(src, srcname, s, tt):
            gt = s * TPS + tt
            xa = x_cur[:, tt * D:(tt + 1) * D]
            ybuf = y_tok[:, :] if tt % 2 == 0 else scr[:, 0:2048]
            yres = ["y_tok"] if tt % 2 == 0 else SCRALL
            c_negm = ln[:, 8 + tt:9 + tt]
            c_rstd = ln[:, 40 + tt:41 + tt]
            kb.op("vector", lambda e, a=xa, o=ybuf, m=c_negm, r=c_rstd:
                  e.tensor_scalar(o, a, m, r, OP.add, OP.mult),
                  reads=[XC[tt], f"negm{tt}", f"rstd{tt}"], writes=yres,
                  sreads=(f"negm{tt}", f"rstd{tt}"))
            for half in range(2):
                bank = 4 + half
                for kk in range(8):
                    k = half * 8 + kk
                    first = (kk % 4 == 0)
                    last = (kk % 4 == 3)
                    pa = P[bank][:, (kk % 4) * 128:(kk % 4) * 128 + 128]
                    ya = ybuf[:, k * 128:(k + 1) * 128]
                    kb.op("tensor", lambda e, pa=pa, ya=ya:
                          e.transpose(pa, ya, id_sb[:, :]),
                          reads=yres + ["ident"],
                          writes=[PK[bank]] if (first or last) else (),
                          inc=last)
                    if last:
                        k0 = k - 3
                        dh = yfh3[:, k0:k0 + 4, tt * 128:(tt + 1) * 128]
                        dl = yfl3[:, k0:k0 + 4, tt * 128:(tt + 1) * 128]
                        srca = P[bank][:, :].rearrange("p (k t) -> p k t", k=4)
                        kb.op("scalar", lambda e, d=dh, sa=srca:
                              e.activation(d, sa, AF.Copy),
                              reads=[PK[bank]], writes=["yfhw"])
                        kb.op("vector", lambda e, d=dl, sa=srca, h=dh:
                              e.tensor_tensor(d, sa, h, OP.subtract),
                              reads=[PK[bank], "yfhw"],
                              writes=[YF[tt]] if k == 15 else ())

        def feeder(units, eng="sync", sem="wd"):
            state = {"next": 0}

            def need(i, limit=None):
                top = i + 1 if limit is None else min(i + 1, limit)
                while state["next"] <= min(top, len(units) - 1):
                    u = state["next"]
                    ap_src, slname, width, srcres = units[u]
                    d_ap = slot_ap[slname][:, 0:width]
                    if len(ap_src.shape) == 3:
                        d_ap = d_ap.rearrange("p (k w) -> p k w",
                                              k=ap_src.shape[1])
                    kb.op(eng, lambda e, d=d_ap, s=ap_src:
                          e.dma_start(out=d, in_=s),
                          reads=srcres, writes=[slname], sem=sem, n=16)
                    state["next"] = u + 1
            return need

        def mm3(bank, lh, ll, rh, rl, reads, first, last, inc_last, pw=512):
            kb.op("tensor", lambda e, pa=P[bank][:, 0:pw], la=lh, ra=rh,
                  st=first: e.matmul(pa, la, ra, start=st, stop=False),
                  reads=reads, writes=[PK[bank]] if first else (), inc=False)
            kb.op("tensor", lambda e, pa=P[bank][:, 0:pw], la=lh, ra=rl:
                  e.matmul(pa, la, ra, start=False, stop=False),
                  reads=reads, inc=False)
            kb.op("tensor", lambda e, pa=P[bank][:, 0:pw], la=ll, ra=rh,
                  sp=last: e.matmul(pa, la, ra, start=False, stop=sp),
                  reads=reads, writes=[PK[bank]] if last else (),
                  inc=inc_last)

        def proj(bank, tt, need, units, i0, bias_ap, bias_res):
            for q in range(8):
                need(i0 + q)
                sl = units[i0 + q][1]
                for kk in range(2):
                    k = 2 * q + kk
                    lh = yfh3[:, k, tt * 128:(tt + 1) * 128]
                    ll = yfl3[:, k, tt * 128:(tt + 1) * 128]
                    rh = slot_ap[sl][:, kk * 1024:kk * 1024 + 512]
                    rl = slot_ap[sl][:, kk * 1024 + 512:kk * 1024 + 1024]
                    mm3(bank, lh=lh, ll=ll, rh=rh, rl=rl,
                        reads=[YF[tt], sl], first=(k == 0), last=False,
                        inc_last=(kk == 1))
            kb.op("tensor", lambda e, pa=P[bank][:, :], ba=bias_ap:
                  e.matmul(pa, ones1[:, :], ba, start=False, stop=True),
                  reads=["ones1", bias_res], writes=[PK[bank]])

        def ffn_phase(src, srcname, dst):
            dstname = DSTN[id(dst)]
            for s in range(S):
                for tt in range(TPS):
                    load_x(src, srcname, s, tt)
                for tt in range(TPS):
                    stage_a_stats(src, srcname, s, tt)
                for tt in range(TPS):
                    stage_a_norm(src, srcname, s, tt)
                gate_units, up_units, down_units = [], [], []
                for j in range(NJ):
                    for half in range(2):
                        for q in range(4):
                            gate_units.append((
                                wgb[j, half, 4 * q:4 * q + 4, :, :].rearrange(
                                    "k p w -> p k w"),
                                SLOT[len(gate_units) % 2], 2048,
                                [f"Bwg_{j}_{2 * q}", f"Bwg_{j}_{2 * q}l",
                                 f"Bwg_{j}_{2 * q + 1}", f"Bwg_{j}_{2 * q + 1}l"]))
                            up_units.append((
                                wub[j, half, 4 * q:4 * q + 4, :, :].rearrange(
                                    "k p w -> p k w"),
                                SLOT[2 + len(up_units) % 2], 2048,
                                [f"Bwu_{j}_{2 * q}", f"Bwu_{j}_{2 * q}l",
                                 f"Bwu_{j}_{2 * q + 1}", f"Bwu_{j}_{2 * q + 1}l"]))
                        dsl_pat = ["s4", "s5", "s2", "s3"]
                        for hh in range(2):
                            for kdx in range(2):
                                kd = half * 2 + kdx
                                down_units.append((
                                    wdb[j, kd, hh, :, :],
                                    dsl_pat[len(down_units) % 4], 2048,
                                    [f"Bwd_{j}_{kd}_{hh}", f"Bwd_{j}_{kd}_{hh}l"]))
                gneed = feeder(gate_units)
                uneed = feeder(up_units)
                dneed = feeder(down_units)
                state = {"gi": 0, "ui": 0, "di": 0}

                def gu_pass(units, need, key, bank_cols=256):
                    i0 = state[key]
                    for q in range(4):
                        need(state[key], limit=i0 + 3)
                        sl = units[state[key]][1]
                        for kk in range(4):
                            k = 4 * q + kk
                            rh = slot_ap[sl][:, kk * 512:kk * 512 + 256]
                            rl = slot_ap[sl][:, kk * 512 + 256:kk * 512 + 512]
                            for t in range(8):
                                lh = yfh3[:, k, t * 128:(t + 1) * 128]
                                ll = yfl3[:, k, t * 128:(t + 1) * 128]
                                mm3(t, lh=lh, ll=ll, rh=rh, rl=rl,
                                    reads=[YF[t], sl],
                                    first=(k == 0), last=(k == 15),
                                    inc_last=(k == 15 or (kk == 3 and t == 7)),
                                    pw=256)
                        state[key] += 1

                for j in range(NJ):
                    for half in range(2):
                        gu_pass(gate_units, gneed, "gi")
                        for t in range(8):
                            kb.op("scalar", lambda e, pa=P[t][:, 0:256],
                                  o=sg[:, t * 256:(t + 1) * 256]:
                                  e.activation(o, pa, AF.Silu),
                                  reads=[PK[t]], writes=["y_tok"])
                        gu_pass(up_units, uneed, "ui")
                        sc04 = scr0[:, :].rearrange("p (t i w) -> p t i w",
                                                    i=4, w=64)
                        dneed(state["di"] + 2, limit=state["di"] + 3)
                        for tp in range(4):
                            t0, t1 = 2 * tp, 2 * tp + 1
                            for t in (t0, t1):
                                kb.op("vector", lambda e, t=t,
                                      pa=P[t][:, 0:256],
                                      s1=sg[:, t * 256:(t + 1) * 256]:
                                      e.tensor_tensor(s1, s1, pa, OP.mult),
                                      reads=["y_tok", PK[t]],
                                      writes=["y_tok"])
                            sga = sg[:, t0 * 256:t0 * 256 + 512]
                            kb.op("scalar", lambda e, a=sga:
                                  e.activation(scr0, a, AF.Abs),
                                  reads=["y_tok"], writes=["scr0"])
                            mv = [sc04[:, 0:2, i, :] for i in range(4)]
                            s2v = scr2[:, :].rearrange("p (t w) -> p t w", w=64)
                            spv = [s2v[:, 2 * i:2 * i + 2, :] for i in range(4)]
                            kb.op("vector", lambda e, m=mv, o=spv[0]:
                                  e.tensor_tensor(o, m[0], m[1], OP.max),
                                  reads=["scr0"], writes=["scr2"])
                            kb.op("vector", lambda e, m=mv, o=spv[1]:
                                  e.tensor_tensor(o, m[0], m[1], OP.min),
                                  reads=["scr0"], writes=["scr2"])
                            kb.op("vector", lambda e, m=mv, o=spv[2]:
                                  e.tensor_tensor(o, m[2], m[3], OP.max),
                                  reads=["scr0"], writes=["scr2"])
                            kb.op("vector", lambda e, m=mv, o=spv[3]:
                                  e.tensor_tensor(o, m[2], m[3], OP.min),
                                  reads=["scr0"], writes=["scr2"])
                            spe = sp_e[:, :].rearrange("p (t w) -> p t w", w=64)[:, 0:2, :]
                            spf = sp_f[:, :].rearrange("p (t w) -> p t w", w=64)[:, 0:2, :]
                            spt = sp_t[:, :].rearrange("p (t w) -> p t w", w=64)[:, 0:2, :]
                            kb.op("vector", lambda e, o=spe, a=spv[0], b=spv[2]:
                                  e.tensor_tensor(o, a, b, OP.min),
                                  reads=["scr2"], writes=["sp_e"])
                            kb.op("vector", lambda e, o=spf, a=spv[1], b=spv[3]:
                                  e.tensor_tensor(o, a, b, OP.max),
                                  reads=["scr2"], writes=["sp_f"])
                            kb.op("vector", lambda e, o=spt, a=spe, b=spf:
                                  e.tensor_tensor(o, a, b, OP.max),
                                  reads=["sp_e", "sp_f"], writes=["sp_t"])
                            s1v = scr1[:, :].rearrange("p (t i w) -> p t i w",
                                                       i=4, w=64)
                            for i in range(4):
                                kb.op("vector", lambda e, i=i, m=mv, c=spt,
                                      o=s1v[:, 0:2, i, :]:
                                      e.tensor_tensor(o, m[i], c, OP.is_ge),
                                      reads=["scr0", "sp_t"], writes=["scr1"])
                            kb.op("vector", lambda e, a=sga:
                                  e.tensor_tensor(a, a, scr1, OP.mult),
                                  reads=["y_tok", "scr1"], writes=["y_tok"])
                            for t in (t0, t1):
                                for i in range(2):
                                    kb.op("tensor", lambda e, i=i, t=t,
                                          pa=P[t][:, i * 128:(i + 1) * 128],
                                          ya=sg[:, t * 256 + i * 128:
                                                t * 256 + (i + 1) * 128]:
                                          e.transpose(pa, ya, id_sb[:, :]),
                                          reads=["y_tok", "ident"],
                                          writes=[PK[t]],
                                          inc=(i == 1))
                                kb.op("scalar", lambda e, t=t, pa=P[t][:, 0:256],
                                      hf=hfm_h[:, t * 256:(t + 1) * 256]:
                                      e.activation(hf, pa, AF.Copy),
                                      reads=[PK[t]], writes=[f"hfh{t}"])
                                kb.op("vector", lambda e, t=t, pa=P[t][:, 0:256],
                                      hf=hfm_h[:, t * 256:(t + 1) * 256],
                                      lf=hfm_l[:, t * 256:(t + 1) * 256]:
                                      e.tensor_tensor(lf, pa, hf, OP.subtract),
                                      reads=[PK[t], f"hfh{t}"],
                                      writes=[f"hfl{t}"])
                            for ob in (0, 1):
                                hho, obx = ob // 2, ob % 2
                                for t in (t0, t1):
                                    for kdx in range(2):
                                        dsl = down_units[state["di"] + 2 * hho + kdx][1]
                                        lh = hfm_h[:, t * 256 + kdx * 128:
                                                   t * 256 + kdx * 128 + 128]
                                        ll = hfm_l[:, t * 256 + kdx * 128:
                                                   t * 256 + kdx * 128 + 128]
                                        rh = slot_ap[dsl][:, obx * 512:obx * 512 + 512]
                                        rl = slot_ap[dsl][:, 1024 + obx * 512:
                                                      1024 + obx * 512 + 512]
                                        mm3(t, lh=lh, ll=ll, rh=rh, rl=rl,
                                            reads=[f"hfh{t}", f"hfl{t}", dsl],
                                            first=(kdx == 0), last=(kdx == 1),
                                            inc_last=(kdx == 1))
                                    xa = x_cur[:, t * D + ob * 512:
                                               t * D + ob * 512 + 512]
                                    kb.op("vector", lambda e, t=t, pa=P[t][:, :],
                                          xa=xa:
                                          e.scalar_tensor_tensor(
                                              xa, pa, 0.5, xa, OP.mult, OP.add),
                                          reads=[PK[t], XC[t]],
                                          writes=[XC[t]])
                        for ob in range(2, 4):
                            hho, obx = ob // 2, ob % 2
                            for t in range(8):
                                for kdx in range(2):
                                    dsl = down_units[state["di"] + 2 * hho + kdx][1]
                                    lh = hfm_h[:, t * 256 + kdx * 128:
                                               t * 256 + kdx * 128 + 128]
                                    ll = hfm_l[:, t * 256 + kdx * 128:
                                               t * 256 + kdx * 128 + 128]
                                    rh = slot_ap[dsl][:, obx * 512:obx * 512 + 512]
                                    rl = slot_ap[dsl][:, 1024 + obx * 512:
                                                  1024 + obx * 512 + 512]
                                    mm3(t, lh=lh, ll=ll, rh=rh, rl=rl,
                                        reads=[f"hfh{t}", f"hfl{t}", dsl],
                                        first=(kdx == 0), last=(kdx == 1),
                                        inc_last=(kdx == 1))
                                xa = x_cur[:, t * D + ob * 512:
                                           t * D + ob * 512 + 512]
                                kb.op("vector", lambda e, t=t, pa=P[t][:, :],
                                      xa=xa:
                                      e.scalar_tensor_tensor(
                                          xa, pa, 0.5, xa, OP.mult, OP.add),
                                      reads=[PK[t], XC[t]],
                                      writes=[XC[t]])
                        state["di"] += 4
                for tt in range(TPS):
                    gt = s * TPS + tt
                    kb.op("scalar", lambda e, xa=x_cur[:, tt * D:(tt + 1) * D],
                          da=dst[gt * 128:(gt + 1) * 128, :]:
                          e.dma_start(out=da, in_=xa),
                          reads=[XC[tt]], writes=[f"{dstname}_{gt}"],
                          sem="xd", n=16)

        def attn_phase():
            veh3 = veh[:, :].rearrange("p (j w) -> p j w", w=132)
            vel3 = vel[:, :].rearrange("p (j w) -> p j w", w=132)
            kb.op("vector", lambda e: e.memset(veh[:, :], 0.0), writes=["veh"])
            kb.op("vector", lambda e: e.memset(veh3[:, :, 128:129], 1.0),
                  writes=["veh"])
            kb.op("vector", lambda e: e.memset(vel[:, :], 0.0), writes=["vel"])
            kv_units, vv_units = [], []
            for gt in range(NT):
                for q in range(8):
                    kv_units.append((
                        wkb[2 * q:2 * q + 2, :, :].rearrange("k p w -> p k w"),
                        SLOT[len(kv_units) % 2], 2048,
                        [f"Bwk_{q}", f"Bwk_{q}l"]))
                    vv_units.append((
                        wvb[2 * q:2 * q + 2, :, :].rearrange("k p w -> p k w"),
                        SLOT[2 + len(vv_units) % 2], 2048,
                        [f"Bwv_{q}", f"Bwv_{q}l"]))
            kneed = feeder(kv_units)
            vneed = feeder(vv_units)
            for s in range(S):
                for tt in range(TPS):
                    load_x(park1, "park1", s, tt)
                for tt in range(TPS):
                    stage_a_stats(park1, "park1", s, tt)
                for tt in range(TPS):
                    gt = s * TPS + tt
                    stage_a_norm(park1, "park1", s, tt)
                    proj(6, tt, kneed, kv_units, 8 * gt, bk_sb[:, :], "bk_sb")
                    proj(7, tt, vneed, vv_units, 8 * gt, bv_sb[:, :], "bv_sb")
                    kb.op("vector", lambda e, pa=P[6][:, :]:
                          e.tensor_scalar(scr2, pa, 0.0, None, OP.min),
                          reads=[PK[6]], writes=["scr2"])
                    kb.op("scalar", lambda e: e.activation(scr1, scr2, AF.Exp),
                          reads=["scr2"], writes=["scr1"])
                    kb.op("vector", lambda e, pa=P[6][:, :]:
                          e.scalar_tensor_tensor(scr0, pa, 0.0,
                                                 scr1, OP.max, OP.add),
                          reads=[PK[6], "scr1"], writes=["scr0"])
                    kb.op("scalar", lambda e: e.activation(
                        skb[:, 0:512], scr0, AF.Copy),
                        reads=["scr0"], writes=["skh"])
                    kb.op("vector", lambda e: e.tensor_tensor(
                        skb[:, 512:1024], scr0, skb[:, 0:512], OP.subtract),
                        reads=["scr0", "skh"], writes=["skl"])
                    kb.op("scalar", lambda e,
                          sa=P[7][:, :].rearrange("p (j w) -> p j w", w=128),
                          da=veh3[:, :, 0:128]:
                          e.activation(da, sa, AF.Copy),
                          reads=[PK[7], "veh"], writes=["vedh"])
                    kb.op("vector", lambda e,
                          sa=P[7][:, :].rearrange("p (j w) -> p j w", w=128),
                          da=vel3[:, :, 0:128], ha=veh3[:, :, 0:128]:
                          e.tensor_tensor(da, sa, ha, OP.subtract),
                          reads=[PK[7], "vedh", "vel"], writes=["vedl"])
                    for jh in range(4):
                        lh = skb[:, jh * 128:(jh + 1) * 128]
                        ll = skb[:, 512 + jh * 128:512 + (jh + 1) * 128]
                        rh = veh[:, jh * 132:jh * 132 + 132]
                        rl = vel[:, jh * 132:jh * 132 + 132]
                        kb.op("tensor", lambda e, pa=P[jh][:, 0:132], la=lh,
                              ra=rh, st=(gt == 0):
                              e.matmul(pa, la, ra, start=st, stop=False),
                              reads=["skh", "veh", "vedh"],
                              writes=[PK[jh]] if gt == 0 else (), inc=False)
                        kb.op("tensor", lambda e, pa=P[jh][:, 0:132], la=lh,
                              ra=rl: e.matmul(pa, la, ra, start=False, stop=False),
                              reads=["skh", "vel", "vedl"], inc=False)
                        kb.op("tensor", lambda e, pa=P[jh][:, 0:132], la=ll,
                              ra=rh, sp=(gt == NT - 1):
                              e.matmul(pa, la, ra, start=False, stop=sp),
                              reads=["skl", "veh", "vedh"],
                              writes=[PK[jh]] if gt == NT - 1 else (),
                              inc=(gt == NT - 1 or jh == 3))
            for jh in range(4):
                kb.op("vector", lambda e, jh=jh, pa=P[jh][:, 0:132]:
                      e.tensor_copy(kv_sb[:, jh * 132:jh * 132 + 132], pa),
                      reads=[PK[jh]], writes=["kv"])
            kb.op("gpsimd", lambda e: e.dma_start(out=cc_in_t.ap(),
                                                  in_=kv_sb[:, :]),
                  reads=["kv"], writes=["cc_in"], sem="gd", n=16)
            kb.op("gpsimd", lambda e: e.collective_compute(
                "AllReduce", OP.add,
                replica_groups=[[0, 1], [2, 3], [4, 5], [6, 7]],
                ins=[cc_in_t.ap().opt()], outs=[cc_out_t.ap().opt()]),
                reads=["cc_in"], writes=["cc_out"], sem="cc", n=1)
            kb.op("gpsimd", lambda e: e.dma_start(out=kv_sb[:, :],
                                                  in_=cc_out_t.ap()),
                  reads=["cc_out"], writes=["kv"], sem="gd", n=16)

            def kv_split():
                kb.op("scalar", lambda e: e.activation(
                    kvb[:, 0:528], kv_sb[:, :], AF.Copy),
                    reads=["kv"], writes=["kvh"])
                kb.op("vector", lambda e: e.tensor_tensor(
                    kvb[:, 528:1056], kv_sb[:, :], kvb[:, 0:528], OP.subtract),
                    reads=["kv", "kvh"], writes=["kvl"])

            q_units, o_units = [], []
            for gt in range(NT):
                for ob in range(NOB):
                    for q in range(8):
                        q_units.append((
                            wqb[ob, 2 * q:2 * q + 2, :, :].rearrange(
                                "k p w -> p k w"),
                            SLOT[len(q_units) % 4], 2048,
                            [f"Bwq_{ob}_{q}", f"Bwq_{ob}_{q}l"]))
                for k in range(KC):
                    for hh in range(2):
                        o_units.append((
                            wob[k, 2 * hh:2 * hh + 2, :, :].rearrange(
                                "o p w -> p o w"),
                            SLOT[4 + len(o_units) % 2], 2048,
                            [f"Bwo_{k}_{hh}", f"Bwo_{k}_{hh}l"]))
            qneed = feeder(q_units)
            oneed = feeder(o_units)
            qi = oi = 0
            for s in range(S):
                for tt in range(TPS):
                    load_x(park1, "park1", s, tt)
                for tt in range(TPS):
                    stage_a_stats(park1, "park1", s, tt)
                for tt in range(TPS):
                    gt = s * TPS + tt
                    stage_a_norm(park1, "park1", s, tt)
                    def p2_chain(ob):
                        qb = 6 + ob % 2
                        kb.op("vector", lambda e, pa=P[qb][:, :]:
                              e.tensor_scalar(scr2, pa, 0.0, None, OP.min),
                              reads=[PK[qb]], writes=["scr2"])
                        kb.op("scalar", lambda e:
                              e.activation(scr1, scr2, AF.Exp),
                              reads=["scr2"], writes=["scr1"])
                        kb.op("vector", lambda e, pa=P[qb][:, :]:
                              e.scalar_tensor_tensor(scr0, pa, 0.0, scr1,
                                                     OP.max, OP.add),
                              reads=[PK[qb], "scr1"], writes=["scr0"])
                        for i in range(4):
                            kb.op("tensor", lambda e, i=i,
                                  pa=P[4][:, i * 128:(i + 1) * 128]:
                                  e.transpose(pa, scr0[:, i * 128:(i + 1) * 128],
                                              id_sb[:, :]),
                                  reads=["scr0", "ident"],
                                  writes=[PK[4]] if (i == 0 or i == 3) else (),
                                  inc=(i == 3))
                        kb.op("scalar", lambda e, pa=P[4][:, :]:
                              e.activation(sqfb[:, 0:512], pa, AF.Copy),
                              reads=[PK[4]], writes=["sqh"])
                        kb.op("vector", lambda e, pa=P[4][:, :]:
                              e.tensor_tensor(sqfb[:, 512:1024], pa,
                                              sqfb[:, 0:512], OP.subtract),
                              reads=[PK[4], "sqh"], writes=["sql"])
                        for i in range(4):
                            lh = sqfb[:, i * 128:(i + 1) * 128]
                            ll = sqfb[:, 512 + i * 128:512 + (i + 1) * 128]
                            rh = kvb[:, ob * 132:ob * 132 + 132]
                            rl = kvb[:, 528 + ob * 132:528 + ob * 132 + 132]
                            kb.op("tensor", lambda e, i=i, pa=P[i][:, 0:132],
                                  la=lh, ra=rh:
                                  e.matmul(pa, la, ra, start=True, stop=False),
                                  reads=["sqh", "kvh"], writes=[PK[i]],
                                  inc=False)
                            kb.op("tensor", lambda e, i=i, pa=P[i][:, 0:132],
                                  la=lh, ra=rl:
                                  e.matmul(pa, la, ra, start=False, stop=False),
                                  reads=["sqh", "kvl"], inc=False)
                            kb.op("tensor", lambda e, i=i, pa=P[i][:, 0:132],
                                  la=ll, ra=rh:
                                  e.matmul(pa, la, ra, start=False, stop=True),
                                  reads=["sql", "kvh"], writes=[PK[i]])
                        for i in range(4):
                            kb.op("vector", lambda e, i=i, pa=P[i][:, 128:129]:
                                  e.tensor_scalar(rr[:, i:i + 1], pa, EPS_AT,
                                                  None, OP.add),
                                  reads=[PK[i]], writes=[f"r1_{i}"], strict=True)
                            kb.op("vector", lambda e, i=i:
                                  e.reciprocal(rr[:, 8 + i:9 + i],
                                               rr[:, i:i + 1]),
                                  reads=[f"r1_{i}"], writes=[f"r2_{i}"],
                                  strict=True)
                            kb.op("vector", lambda e, i=i, pa=P[i][:, 0:128]:
                                  e.tensor_scalar(scr2[:, i * 128:(i + 1) * 128],
                                                  pa, rr[:, 8 + i:9 + i], None,
                                                  OP.mult),
                                  reads=[PK[i], f"r2_{i}"], writes=["scr2"],
                                  sreads=(f"r2_{i}",))
                        for i in range(4):
                            kb.op("tensor", lambda e, i=i,
                                  pa=P[5][:, i * 128:(i + 1) * 128]:
                                  e.transpose(pa, scr2[:, i * 128:(i + 1) * 128],
                                              id_sb[:, :]),
                                  reads=["scr2", "ident"],
                                  writes=[PK[5]] if (i == 0 or i == 3) else (),
                                  inc=(i == 3))
                        kb.op("scalar", lambda e, pa=P[5][:, :],
                              da=afh[:, ob * 512:ob * 512 + 512]:
                              e.activation(da, pa, AF.Copy),
                              reads=[PK[5]], writes=[f"afh{ob}"])
                        kb.op("vector", lambda e, pa=P[5][:, :],
                              da=afl[:, ob * 512:ob * 512 + 512],
                              ha=afh[:, ob * 512:ob * 512 + 512]:
                              e.tensor_tensor(da, pa, ha, OP.subtract),
                              reads=[PK[5], f"afh{ob}"], writes=[f"afl{ob}"])

                    for ob in range(NOB):
                        proj(6 + ob % 2, tt, qneed, q_units, qi,
                             bq_sb[:, ob * 512:ob * 512 + 512], "bq_sb")
                        qi += 8
                        if s == 0 and tt == 0 and ob == 0:
                            kv_split()
                        if ob >= 1:
                            p2_chain(ob - 1)
                    p2_chain(3)
                    AFR = ["afh0", "afh1", "afh2", "afh3",
                           "afl0", "afl1", "afl2", "afl3"]
                    for k in range(KC):
                        for hh in range(2):
                            oneed(oi)
                            osl = o_units[oi][1]
                            for obx in range(2):
                                ob = 2 * hh + obx
                                bank = 4 + ob
                                lh = afh[:, k * 128:(k + 1) * 128]
                                ll = afl[:, k * 128:(k + 1) * 128]
                                rh = slot_ap[osl][:, obx * 1024:obx * 1024 + 512]
                                rl = slot_ap[osl][:, obx * 1024 + 512:obx * 1024 + 1024]
                                mm3(bank, lh=lh, ll=ll, rh=rh, rl=rl,
                                    reads=AFR + [osl],
                                    first=(k == 0), last=(k == KC - 1),
                                    inc_last=(k == KC - 1 or obx == 1))
                            oi += 1
                    for ob in range(NOB):
                        bank = 4 + ob
                        xa = x_cur[:, tt * D + ob * 512:tt * D + ob * 512 + 512]
                        kb.op("vector", lambda e, pa=P[bank][:, :], xa=xa:
                              e.tensor_tensor(xa, xa, pa, OP.add),
                              reads=[PK[bank], XC[tt]], writes=[XC[tt]])
                    kb.op("scalar", lambda e, xa=x_cur[:, tt * D:(tt + 1) * D],
                          da=park2[gt * 128:(gt + 1) * 128, :]:
                          e.dma_start(out=da, in_=xa),
                          reads=[XC[tt]], writes=[f"park2_{gt}"],
                          sem="xd", n=16)

        ffn_phase(x_in, None, park1)
        attn_phase()
        ffn_phase(park2, "park2", outd)

        kb._wait("scalar", "xd", kb.cnt["xd"])
        kb._wait("gpsimd", "gd", kb.cnt["gd"])
        kb._wait("sync", "wd", kb.cnt["wd"])

        with nc.Block() as block:
            @block.sync
            def _(e):
                for f in kb.ops["sync"]:
                    f(e)

            @block.scalar
            def _(e):
                for f in kb.ops["scalar"]:
                    f(e)

            @block.vector
            def _(e):
                for f in kb.ops["vector"]:
                    f(e)

            @block.tensor
            def _(e):
                for f in kb.ops["tensor"]:
                    f(e)

            @block.gpsimd
            def _(e):
                for f in kb.ops["gpsimd"]:
                    f(e)

    return nc


def _host_prep(inputs):
    P512 = np.empty(512, np.int64)
    g = np.arange(64)
    for h in range(2):
        for i in range(4):
            P512[h * 256 + i * 64:h * 256 + (i + 1) * 64] = 4 * (64 * h + g) + i

    w_gate = np.asarray(inputs["L0_ffn_w_gate"], np.float32)
    w_up = np.asarray(inputs["L0_ffn_w_up"], np.float32)
    w_down = np.asarray(inputs["L0_ffn_w_down"], np.float32)
    Wq = np.asarray(inputs["L0_attn_Wq"], np.float32)
    Wk = np.asarray(inputs["L0_attn_Wk"], np.float32)
    Wv = np.asarray(inputs["L0_attn_Wv"], np.float32)
    Wo = np.asarray(inputs["L0_attn_Wo"], np.float32)

    wg_h = np.ascontiguousarray(
        w_gate.reshape(D, NJ, JW)[:, :, P512].transpose(1, 0, 2)
    ).reshape(NJ, KC, 128, JW)
    wu_h = np.ascontiguousarray(
        w_up.reshape(D, NJ, JW)[:, :, P512].transpose(1, 0, 2)
    ).reshape(NJ, KC, 128, JW)
    wdn_h = np.ascontiguousarray(
        w_down.reshape(NJ, JW, D)[:, P512, :]).reshape(NJ, 4, 128, D)
    wq_h = np.ascontiguousarray(
        Wq.T.reshape(KC, 128, NOB, JW).transpose(2, 0, 1, 3))
    wo_h = np.ascontiguousarray(Wo.T.reshape(KC, 128, NOB, JW).transpose(0, 2, 1, 3))
    wk_h = np.ascontiguousarray(Wk.T.reshape(KC, 128, JW))
    wv_h = np.ascontiguousarray(Wv.T.reshape(KC, 128, JW))

    base = {
        "wg": wg_h, "wu": wu_h, "wdn": wdn_h, "wq": wq_h, "wo": wo_h,
        "wk": wk_h, "wv": wv_h,
        "bq": np.ascontiguousarray(
            np.asarray(inputs["L0_attn_bq"], np.float32).reshape(1, D)),
        "bk": np.ascontiguousarray(
            np.asarray(inputs["L0_attn_bk"], np.float32).reshape(1, JW)),
        "bv": np.ascontiguousarray(
            np.asarray(inputs["L0_attn_bv"], np.float32).reshape(1, JW)),
        "ident": np.eye(128, dtype=np.float32),
    }
    x = np.asarray(inputs["L0_attn_hidden_states"], np.float32)
    in_maps = []
    for c in range(8):
        m = dict(base)
        m["x_in"] = np.ascontiguousarray(
            x[c // 2, (c % 2) * T:(c % 2 + 1) * T, :])
        in_maps.append(m)
    return in_maps


def kernel(**inputs):
    in_maps = _host_prep(inputs)
    nc = build_program()
    res = run_bass_kernel_spmd(nc, in_maps, core_ids=list(range(8)))
    outp = np.empty((4, 4096, D), np.float32)
    for c in range(8):
        outp[c // 2, (c % 2) * T:(c % 2 + 1) * T, :] = res.results[c]["out"]
    return outp


# version 19
# speedup vs baseline: 1.0537x; 1.0537x over previous
import numpy as np
from contextlib import ExitStack

import concourse.bass as bass
import concourse.mybir as mybir
from concourse.alu_op_type import AluOpType as OP
from concourse.bass_utils import run_bass_kernel_spmd

F32 = mybir.dt.float32
BF16 = mybir.dt.bfloat16
AF = mybir.ActivationFunctionType
AX = mybir.AxisListType.X

D = 2048
T = 2048
NT = 16
S = 2
TPS = 8
KC = 16
NJ = 16
JW = 512
NOB = 4
EPS_LN = 1e-5
EPS_AT = 1e-6

ENGINES = ("sync", "scalar", "vector", "tensor", "gpsimd")
DEFSEM = {"sync": "wd", "scalar": "act", "vector": "dve", "tensor": "pe",
          "gpsimd": "gp"}
SEMS = ("wd", "xd", "gd", "pe", "dve", "act", "gp", "cc")


class KB:
    def __init__(self):
        self.ops = {e: [] for e in ENGINES}
        self.cnt = {s: 0 for s in SEMS}
        self.lastw = {e: {s: 0 for s in SEMS} for e in ENGINES}
        self.ready = {}
        self.free = {}
        self.pending = {e: [] for e in ENGINES}
        self.strictset = set()
        self.semh = None

    def _wait(self, eng, sem, val):
        if val <= 0 or self.lastw[eng][sem] >= val:
            return
        assert val <= self.cnt[sem], f"wait {sem}={val} > emitted {self.cnt[sem]}"
        self.lastw[eng][sem] = val

        def f(e, s=sem, v=val):
            e.wait_ge(self.semh[s], v)
        self.ops[eng].append(f)

    def op(self, eng, fn, reads=(), writes=(), sem=None, n=1, inc=True,
           strict=False, sreads=()):
        own = DEFSEM[eng] if sem is None else "__dma__"
        for r in reads:
            if r in self.ready:
                s2, v2 = self.ready[r]
                if s2 != own or r in self.strictset or r in sreads:
                    self._wait(eng, s2, v2)
        for r in sreads:
            if r in self.ready:
                self._wait(eng, *self.ready[r])
        for w in writes:
            for s2, v2 in self.free.get(w, {}).items():
                if s2 != own:
                    self._wait(eng, s2, v2)
            if w in self.ready:
                s2, v2 = self.ready[w]
                if s2 != own:
                    self._wait(eng, s2, v2)
        if not inc:
            self.ops[eng].append(fn)
            self.pending[eng].extend(reads)
            return None
        if sem is None:
            sem = DEFSEM[eng]
        self.cnt[sem] += n
        val = self.cnt[sem]

        def f(e, fn=fn, s=sem, n=n):
            fn(e).then_inc(self.semh[s], n)
        self.ops[eng].append(f)
        for w in writes:
            self.ready[w] = (sem, val)
            self.free[w] = {}
            if strict:
                self.strictset.add(w)
            else:
                self.strictset.discard(w)
        for r in list(reads) + self.pending[eng]:
            self.free.setdefault(r, {})[sem] = val
        self.pending[eng] = []
        return val


def build_program():
    nc = bass.Bass("TRN2", target_bir_lowering=False)
    x_in = nc.dram_tensor("x_in", [T, D], F32, kind="ExternalInput").ap()
    wg = nc.dram_tensor("wg", [NJ, KC, 128, JW], F32, kind="ExternalInput").ap()
    wu = nc.dram_tensor("wu", [NJ, KC, 128, JW], F32, kind="ExternalInput").ap()
    wdn = nc.dram_tensor("wdn", [NJ, 4, 128, D], F32, kind="ExternalInput").ap()
    wq = nc.dram_tensor("wq", [NOB, KC, 128, JW], F32, kind="ExternalInput").ap()
    wo = nc.dram_tensor("wo", [KC, NOB, 128, JW], F32, kind="ExternalInput").ap()
    wk = nc.dram_tensor("wk", [KC, 128, JW], F32, kind="ExternalInput").ap()
    wv = nc.dram_tensor("wv", [KC, 128, JW], F32, kind="ExternalInput").ap()
    bqd = nc.dram_tensor("bq", [1, D], F32, kind="ExternalInput").ap()
    bkd = nc.dram_tensor("bk", [1, JW], F32, kind="ExternalInput").ap()
    bvd = nc.dram_tensor("bv", [1, JW], F32, kind="ExternalInput").ap()
    identd = nc.dram_tensor("ident", [128, 128], F32, kind="ExternalInput").ap()
    outd = nc.dram_tensor("out", [T, D], F32, kind="ExternalOutput").ap()
    park1 = nc.dram_tensor("park1", [T, D], F32).ap()
    park2 = nc.dram_tensor("park2", [T, D], F32).ap()
    cc_in_t = nc.dram_tensor("cc_in", [128, 528], F32)
    cc_out_t = nc.dram_tensor("cc_out", [128, 528], F32)
    wgb = nc.dram_tensor("wgb", [NJ, 2, KC, 128, 512], BF16).ap()
    wub = nc.dram_tensor("wub", [NJ, 2, KC, 128, 512], BF16).ap()
    wdb = nc.dram_tensor("wdb", [NJ, 4, 2, 128, 2048], BF16).ap()
    wqb = nc.dram_tensor("wqb", [NOB, KC, 128, 1024], BF16).ap()
    wob = nc.dram_tensor("wob", [KC, NOB, 128, 1024], BF16).ap()
    wkb = nc.dram_tensor("wkb", [KC, 128, 1024], BF16).ap()
    wvb = nc.dram_tensor("wvb", [KC, 128, 1024], BF16).ap()
    DSTN = {id(park1): "park1", id(park2): "park2", id(outd): "out"}

    kb = KB()
    with ExitStack() as es:
        def sb(name, cols, dt=F32, parts=128):
            return es.enter_context(nc.sbuf_tensor(name, [parts, cols], dt))
        x_cur = sb("x_cur", TPS * D)
        yfh = sb("yfh", KC * 1024, BF16)
        yfl = sb("yfl", KC * 1024, BF16)
        rg = sb("rg", 4096, BF16)
        ru = sb("ru", 4096, BF16)
        rd = sb("rd", 4096, BF16)
        y_tok = sb("y_tok", D)
        scr = sb("scr", 2048)
        hfmb = sb("hfmb", 4096, BF16)
        afh = sb("afh", D, BF16)
        afl = sb("afl", D, BF16)
        skb = sb("skb", 1024, BF16)
        veh = sb("veh", 528, BF16)
        vel = sb("vel", 528, BF16)
        kv_sb = sb("kv_sb", 528)
        kvb = sb("kvb", 1056, BF16)
        sqfb = sb("sqfb", 1024, BF16)
        ln = sb("ln", 64)
        rr = sb("rr", 32)
        id_sb = sb("id_sb", 128)
        ones1 = sb("ones1", 128, parts=1)
        bq_sb = sb("bq_sb", D, parts=1)
        bk_sb = sb("bk_sb", JW, parts=1)
        bv_sb = sb("bv_sb", JW, parts=1)
        P = [es.enter_context(nc.psum_tensor(f"P{i}", [128, 512], F32))
             for i in range(8)]
        semh = {s: es.enter_context(nc.semaphore(s)) for s in SEMS}
        kb.semh = semh

        scr0 = scr[:, 0:512]
        scr1 = scr[:, 512:1024]
        scr2 = scr[:, 1024:1536]
        sp_e = scr[:, 1536:1664]
        sp_f = scr[:, 1664:1792]
        sp_t = scr[:, 1792:1920]
        SCRALL = ["scr0", "scr1", "scr2", "sp_e", "sp_f", "sp_t", "scrx"]
        hfm_h = hfmb[:, 0:2048]
        hfm_l = hfmb[:, 2048:4096]
        sg = y_tok

        XC = [f"xc{i}" for i in range(TPS)]
        YF = [f"yf{i}" for i in range(TPS)]
        PK = [f"P{i}" for i in range(8)]
        SLOT = ["s0", "s1", "s2", "s3", "s4", "s5"]
        slot_ap = {"s0": rg[:, 0:2048], "s1": rg[:, 2048:4096],
                   "s2": ru[:, 0:2048], "s3": ru[:, 2048:4096],
                   "s4": rd[:, 0:2048], "s5": rd[:, 2048:4096]}
        yfh3 = yfh[:, :].rearrange("p (k t) -> p k t", k=KC)
        yfl3 = yfl[:, :].rearrange("p (k t) -> p k t", k=KC)

        for apd, aps, res in ((bqd, bq_sb, "bq_sb"), (bkd, bk_sb, "bk_sb"),
                              (bvd, bv_sb, "bv_sb"), (identd, id_sb, "ident")):
            kb.op("scalar", lambda e, a=aps[:, :], r=apd[:, :]:
                  e.dma_start(out=a, in_=r), writes=[res], sem="xd", n=16)
        kb.op("vector", lambda e: e.memset(ones1[:, :], 1.0), writes=["ones1"])

        def presplit():
            chunks = []
            for j in range(NJ):
                for q in range(8):
                    chunks.append((
                        wg[j, 2 * q:2 * q + 2, :, :], 512,
                        wgb[j, :, 2 * q:2 * q + 2, :, 0:256],
                        wgb[j, :, 2 * q:2 * q + 2, :, 256:512], f"Bwg_{j}_{q}"))
                    chunks.append((
                        wu[j, 2 * q:2 * q + 2, :, :], 512,
                        wub[j, :, 2 * q:2 * q + 2, :, 0:256],
                        wub[j, :, 2 * q:2 * q + 2, :, 256:512], f"Bwu_{j}_{q}"))
                for kd in range(4):
                    for hh in range(2):
                        chunks.append((
                            wdn[j, kd, :, hh * 1024:(hh + 1) * 1024], 1024,
                            wdb[j, kd, hh, :, 0:1024],
                            wdb[j, kd, hh, :, 1024:2048], f"Bwd_{j}_{kd}_{hh}"))
            for q in range(8):
                chunks.append((
                    wk[2 * q:2 * q + 2, :, :], 512,
                    wkb[2 * q:2 * q + 2, :, 0:512],
                    wkb[2 * q:2 * q + 2, :, 512:1024], f"Bwk_{q}"))
                chunks.append((
                    wv[2 * q:2 * q + 2, :, :], 512,
                    wvb[2 * q:2 * q + 2, :, 0:512],
                    wvb[2 * q:2 * q + 2, :, 512:1024], f"Bwv_{q}"))
            for ob in range(NOB):
                for q in range(8):
                    chunks.append((
                        wq[ob, 2 * q:2 * q + 2, :, :], 512,
                        wqb[ob, 2 * q:2 * q + 2, :, 0:512],
                        wqb[ob, 2 * q:2 * q + 2, :, 512:1024], f"Bwq_{ob}_{q}"))
            for k in range(KC):
                for hh in range(2):
                    chunks.append((
                        wo[k, 2 * hh:2 * hh + 2, :, :], 512,
                        wob[k, 2 * hh:2 * hh + 2, :, 0:512],
                        wob[k, 2 * hh:2 * hh + 2, :, 512:1024], f"Bwo_{k}_{hh}"))
            fslots = [(x_cur[:, 12288 + 1024 * i:13312 + 1024 * i],
                       "xc6" if i < 2 else "xc7") for i in range(4)]
            hslots = [(skb[:, :], ["skh", "skl"]), (sqfb[:, :], ["sqh", "sql"])]
            lslots = [(afh[:, 0:1024], ["afh0", "afh1"]),
                      (afh[:, 1024:2048], ["afh2", "afh3"])]
            for i, (srcA, w, hiA, loA, res) in enumerate(chunks):
                nd = len(srcA.shape)
                src3 = srcA.rearrange("k p w -> p k w") if nd == 3 else srcA
                fap, rf = fslots[i % 4]
                hap, rhl = hslots[i % 2]
                lap, rll = lslots[i % 2]
                f3 = fap.rearrange("p (k w) -> p k w", w=w) if nd == 3 else fap
                if len(hiA.shape) == 4:
                    kb.op("sync", lambda e, d=f3, s=src3:
                          e.dma_start(out=d, in_=s),
                          writes=[rf], sem="wd", n=16)
                    kb.op("scalar", lambda e, o=hap, a=fap:
                          e.activation(o, a, AF.Copy),
                          reads=[rf], writes=rhl)
                    kb.op("vector", lambda e, o=lap, a=fap, h=hap:
                          e.tensor_tensor(o, a, h, OP.subtract),
                          reads=[rf] + rhl, writes=rll)
                    hv = hap.rearrange("p (k h w) -> p k h w", k=2, h=2)
                    lv = lap.rearrange("p (k h w) -> p k h w", k=2, h=2)
                    for hx in range(2):
                        dh = hiA[hx, :, :, :].rearrange("k p w -> p k w")
                        dl = loA[hx, :, :, :].rearrange("k p w -> p k w")
                        kb.op("scalar", lambda e, d=dh, s=hv[:, :, hx, :]:
                              e.dma_start(out=d, in_=s),
                              reads=rhl, writes=[res] if hx == 1 else (),
                              sem="xd", n=16)
                        kb.op("scalar", lambda e, d=dl, s=lv[:, :, hx, :]:
                              e.dma_start(out=d, in_=s),
                              reads=rll, writes=[res + "l"] if hx == 1 else (),
                              sem="xd", n=16)
                    continue
                if True:
                    hi3 = hiA.rearrange("k p w -> p k w") if nd == 3 else hiA
                    lo3 = loA.rearrange("k p w -> p k w") if nd == 3 else loA
                    h3 = hap.rearrange("p (k w) -> p k w", w=w) if nd == 3 else hap
                    l3 = lap.rearrange("p (k w) -> p k w", w=w) if nd == 3 else lap
                kb.op("sync", lambda e, d=f3, s=src3: e.dma_start(out=d, in_=s),
                      writes=[rf], sem="wd", n=16)
                kb.op("scalar", lambda e, o=hap, a=fap:
                      e.activation(o, a, AF.Copy),
                      reads=[rf], writes=rhl)
                kb.op("vector", lambda e, o=lap, a=fap, h=hap:
                      e.tensor_tensor(o, a, h, OP.subtract),
                      reads=[rf] + rhl, writes=rll)
                kb.op("scalar", lambda e, d=hi3, s=h3: e.dma_start(out=d, in_=s),
                      reads=rhl, writes=[res], sem="xd", n=16)
                kb.op("scalar", lambda e, d=lo3, s=l3: e.dma_start(out=d, in_=s),
                      reads=rll, writes=[res + "l"], sem="xd", n=16)

        presplit()

        def load_x(src, srcname, s, tt):
            gt = s * TPS + tt
            xa = x_cur[:, tt * D:(tt + 1) * D]
            srcres = [] if srcname is None else [f"{srcname}_{gt}"]
            kb.op("scalar", lambda e, a=xa, r=src[gt * 128:(gt + 1) * 128, :]:
                  e.dma_start(out=a, in_=r),
                  reads=srcres, writes=[XC[tt]], sem="xd", n=16)

        def stage_a_stats(src, srcname, s, tt):
            gt = s * TPS + tt
            xa = x_cur[:, tt * D:(tt + 1) * D]
            ybuf = y_tok[:, :] if tt % 2 == 0 else scr[:, 0:2048]
            yres = ["y_tok"] if tt % 2 == 0 else SCRALL
            c_sum = ln[:, tt:tt + 1]
            c_negm = ln[:, 8 + tt:9 + tt]
            c_ssq = ln[:, 16 + tt:17 + tt]
            c_t0 = ln[:, 24 + tt:25 + tt]
            c_t1 = ln[:, 32 + tt:33 + tt]
            c_rstd = ln[:, 40 + tt:41 + tt]
            kb.op("vector", lambda e, a=xa, o=c_sum: e.reduce_sum(o, a, axis=AX),
                  reads=[XC[tt]], writes=[f"lnsum{tt}"], strict=True)
            kb.op("vector", lambda e, i=c_sum, o=c_negm:
                  e.tensor_scalar(o, i, -1.0 / D, None, OP.mult),
                  reads=[f"lnsum{tt}"], writes=[f"negm{tt}"], strict=True)
            kb.op("scalar", lambda e, a=xa, o=ybuf, b=c_negm, acc=c_ssq:
                  e.activation(o, a, AF.Square, bias=b, scale=1.0, accum_out=acc),
                  reads=[XC[tt], f"negm{tt}"], writes=yres + [f"ssq{tt}"])
            kb.op("vector", lambda e, i=c_ssq, o=c_t0:
                  e.tensor_scalar(o, i, 1.0 / D, EPS_LN, OP.mult, OP.add),
                  reads=[f"ssq{tt}"], writes=[f"lt0{tt}"], strict=True)
            kb.op("vector", lambda e, i=c_t0, o=c_t1: e.reciprocal(o, i),
                  reads=[f"lt0{tt}"], writes=[f"lt1{tt}"], strict=True)
            kb.op("scalar", lambda e, i=c_t1, o=c_rstd:
                  e.activation(o, i, AF.Sqrt), reads=[f"lt1{tt}"], writes=[f"rstd{tt}"])

        def stage_a_norm(src, srcname, s, tt):
            gt = s * TPS + tt
            xa = x_cur[:, tt * D:(tt + 1) * D]
            ybuf = y_tok[:, :] if tt % 2 == 0 else scr[:, 0:2048]
            yres = ["y_tok"] if tt % 2 == 0 else SCRALL
            c_negm = ln[:, 8 + tt:9 + tt]
            c_rstd = ln[:, 40 + tt:41 + tt]
            kb.op("vector", lambda e, a=xa, o=ybuf, m=c_negm, r=c_rstd:
                  e.tensor_scalar(o, a, m, r, OP.add, OP.mult),
                  reads=[XC[tt], f"negm{tt}", f"rstd{tt}"], writes=yres,
                  sreads=(f"negm{tt}", f"rstd{tt}"))
            for half in range(2):
                bank = 4 + half
                for kk in range(8):
                    k = half * 8 + kk
                    first = (kk % 4 == 0)
                    last = (kk % 4 == 3)
                    pa = P[bank][:, (kk % 4) * 128:(kk % 4) * 128 + 128]
                    ya = ybuf[:, k * 128:(k + 1) * 128]
                    kb.op("tensor", lambda e, pa=pa, ya=ya:
                          e.transpose(pa, ya, id_sb[:, :]),
                          reads=yres + ["ident"],
                          writes=[PK[bank]] if (first or last) else (),
                          inc=last)
                    if last:
                        k0 = k - 3
                        dh = yfh3[:, k0:k0 + 4, tt * 128:(tt + 1) * 128]
                        dl = yfl3[:, k0:k0 + 4, tt * 128:(tt + 1) * 128]
                        srca = P[bank][:, :].rearrange("p (k t) -> p k t", k=4)
                        kb.op("scalar", lambda e, d=dh, sa=srca:
                              e.activation(d, sa, AF.Copy),
                              reads=[PK[bank]], writes=["yfhw"])
                        kb.op("vector", lambda e, d=dl, sa=srca, h=dh:
                              e.tensor_tensor(d, sa, h, OP.subtract),
                              reads=[PK[bank], "yfhw"],
                              writes=[YF[tt]] if k == 15 else ())

        def feeder(units, eng="sync", sem="wd"):
            state = {"next": 0}

            def need(i, limit=None):
                top = i + 1 if limit is None else min(i + 1, limit)
                while state["next"] <= min(top, len(units) - 1):
                    u = state["next"]
                    ap_src, slname, width, srcres = units[u]
                    d_ap = slot_ap[slname][:, 0:width]
                    if len(ap_src.shape) == 3:
                        d_ap = d_ap.rearrange("p (k w) -> p k w",
                                              k=ap_src.shape[1])
                    kb.op(eng, lambda e, d=d_ap, s=ap_src:
                          e.dma_start(out=d, in_=s),
                          reads=srcres, writes=[slname], sem=sem, n=16)
                    state["next"] = u + 1
            return need

        def mm3(bank, lh, ll, rh, rl, reads, first, last, inc_last, pw=512):
            kb.op("tensor", lambda e, pa=P[bank][:, 0:pw], la=lh, ra=rh,
                  st=first: e.matmul(pa, la, ra, start=st, stop=False),
                  reads=reads, writes=[PK[bank]] if first else (), inc=False)
            kb.op("tensor", lambda e, pa=P[bank][:, 0:pw], la=lh, ra=rl:
                  e.matmul(pa, la, ra, start=False, stop=False),
                  reads=reads, inc=False)
            kb.op("tensor", lambda e, pa=P[bank][:, 0:pw], la=ll, ra=rh,
                  sp=last: e.matmul(pa, la, ra, start=False, stop=sp),
                  reads=reads, writes=[PK[bank]] if last else (),
                  inc=inc_last)

        def proj(bank, tt, need, units, i0, bias_ap, bias_res):
            for q in range(8):
                need(i0 + q)
                sl = units[i0 + q][1]
                for kk in range(2):
                    k = 2 * q + kk
                    lh = yfh3[:, k, tt * 128:(tt + 1) * 128]
                    ll = yfl3[:, k, tt * 128:(tt + 1) * 128]
                    rh = slot_ap[sl][:, kk * 1024:kk * 1024 + 512]
                    rl = slot_ap[sl][:, kk * 1024 + 512:kk * 1024 + 1024]
                    mm3(bank, lh=lh, ll=ll, rh=rh, rl=rl,
                        reads=[YF[tt], sl], first=(k == 0), last=False,
                        inc_last=(kk == 1))
            kb.op("tensor", lambda e, pa=P[bank][:, :], ba=bias_ap:
                  e.matmul(pa, ones1[:, :], ba, start=False, stop=True),
                  reads=["ones1", bias_res], writes=[PK[bank]])

        def ffn_phase(src, srcname, dst):
            dstname = DSTN[id(dst)]
            for s in range(S):
                for tt in range(TPS):
                    load_x(src, srcname, s, tt)
                for tt in range(TPS):
                    stage_a_stats(src, srcname, s, tt)
                for tt in range(TPS):
                    stage_a_norm(src, srcname, s, tt)
                gate_units, up_units, down_units = [], [], []
                for j in range(NJ):
                    for half in range(2):
                        for q in range(4):
                            gate_units.append((
                                wgb[j, half, 4 * q:4 * q + 4, :, :].rearrange(
                                    "k p w -> p k w"),
                                SLOT[len(gate_units) % 2], 2048,
                                [f"Bwg_{j}_{2 * q}", f"Bwg_{j}_{2 * q}l",
                                 f"Bwg_{j}_{2 * q + 1}", f"Bwg_{j}_{2 * q + 1}l"]))
                            up_units.append((
                                wub[j, half, 4 * q:4 * q + 4, :, :].rearrange(
                                    "k p w -> p k w"),
                                SLOT[2 + len(up_units) % 2], 2048,
                                [f"Bwu_{j}_{2 * q}", f"Bwu_{j}_{2 * q}l",
                                 f"Bwu_{j}_{2 * q + 1}", f"Bwu_{j}_{2 * q + 1}l"]))
                        dsl_pat = ["s4", "s5", "s2", "s3"]
                        for hh in range(2):
                            for kdx in range(2):
                                kd = half * 2 + kdx
                                down_units.append((
                                    wdb[j, kd, hh, :, :],
                                    dsl_pat[len(down_units) % 4], 2048,
                                    [f"Bwd_{j}_{kd}_{hh}", f"Bwd_{j}_{kd}_{hh}l"]))
                gneed = feeder(gate_units)
                uneed = feeder(up_units)
                dneed = feeder(down_units)
                state = {"gi": 0, "ui": 0, "di": 0}

                def gu_pass(units, need, key, bank_cols=256):
                    i0 = state[key]
                    for q in range(4):
                        need(state[key], limit=i0 + 3)
                        sl = units[state[key]][1]
                        for kk in range(4):
                            k = 4 * q + kk
                            rh = slot_ap[sl][:, kk * 512:kk * 512 + 256]
                            rl = slot_ap[sl][:, kk * 512 + 256:kk * 512 + 512]
                            for t in range(8):
                                lh = yfh3[:, k, t * 128:(t + 1) * 128]
                                ll = yfl3[:, k, t * 128:(t + 1) * 128]
                                mm3(t, lh=lh, ll=ll, rh=rh, rl=rl,
                                    reads=[YF[t], sl],
                                    first=(k == 0), last=(k == 15),
                                    inc_last=(k == 15 or (kk == 3 and t == 7)),
                                    pw=256)
                        state[key] += 1

                for j in range(NJ):
                    for half in range(2):
                        gu_pass(gate_units, gneed, "gi")
                        for t in range(8):
                            kb.op("scalar", lambda e, pa=P[t][:, 0:256],
                                  o=sg[:, t * 256:(t + 1) * 256]:
                                  e.activation(o, pa, AF.Silu),
                                  reads=[PK[t]], writes=["y_tok"])
                        gu_pass(up_units, uneed, "ui")
                        sc04 = scr0[:, :].rearrange("p (t i w) -> p t i w",
                                                    i=4, w=64)
                        dneed(state["di"] + 2, limit=state["di"] + 3)
                        for tp in range(4):
                            t0, t1 = 2 * tp, 2 * tp + 1
                            for t in (t0, t1):
                                kb.op("vector", lambda e, t=t,
                                      pa=P[t][:, 0:256],
                                      s1=sg[:, t * 256:(t + 1) * 256]:
                                      e.tensor_tensor(s1, s1, pa, OP.mult),
                                      reads=["y_tok", PK[t]],
                                      writes=["y_tok"])
                            sga = sg[:, t0 * 256:t0 * 256 + 512]
                            abs_im = scr0[:, :].rearrange(
                                "p (i t w) -> p t i w", i=4, t=2)
                            kb.op("scalar", lambda e, a=sga, o=abs_im:
                                  e.activation(o, a, AF.Abs),
                                  reads=["y_tok"], writes=["scr0"])
                            mv = [scr0[:, i * 128:(i + 1) * 128]
                                  for i in range(4)]
                            spv = [scr2[:, i * 128:(i + 1) * 128]
                                   for i in range(4)]
                            kb.op("vector", lambda e, m=mv, o=spv[0]:
                                  e.tensor_tensor(o, m[0], m[1], OP.max),
                                  reads=["scr0"], writes=["scr2"])
                            kb.op("vector", lambda e, m=mv, o=spv[1]:
                                  e.tensor_tensor(o, m[0], m[1], OP.min),
                                  reads=["scr0"], writes=["scr2"])
                            kb.op("vector", lambda e, m=mv, o=spv[2]:
                                  e.tensor_tensor(o, m[2], m[3], OP.max),
                                  reads=["scr0"], writes=["scr2"])
                            kb.op("vector", lambda e, m=mv, o=spv[3]:
                                  e.tensor_tensor(o, m[2], m[3], OP.min),
                                  reads=["scr0"], writes=["scr2"])
                            kb.op("vector", lambda e, a=spv[0], b=spv[2]:
                                  e.tensor_tensor(sp_e, a, b, OP.min),
                                  reads=["scr2"], writes=["sp_e"])
                            kb.op("vector", lambda e, a=spv[1], b=spv[3]:
                                  e.tensor_tensor(sp_f, a, b, OP.max),
                                  reads=["scr2"], writes=["sp_f"])
                            kb.op("vector", lambda e:
                                  e.tensor_tensor(sp_t, sp_e, sp_f, OP.max),
                                  reads=["sp_e", "sp_f"], writes=["sp_t"])
                            for i in range(4):
                                kb.op("vector", lambda e, i=i, m=mv,
                                      o=scr1[:, i * 128:(i + 1) * 128]:
                                      e.tensor_tensor(o, m[i], sp_t, OP.is_ge),
                                      reads=["scr0", "sp_t"], writes=["scr1"])
                            ge_tm = scr1[:, :].rearrange(
                                "p (i t w) -> p t i w", i=4, t=2)
                            kb.op("vector", lambda e, a=sga, g=ge_tm:
                                  e.tensor_tensor(a, a, g, OP.mult),
                                  reads=["y_tok", "scr1"], writes=["y_tok"])
                            for t in (t0, t1):
                                for i in range(2):
                                    kb.op("tensor", lambda e, i=i, t=t,
                                          pa=P[t][:, i * 128:(i + 1) * 128],
                                          ya=sg[:, t * 256 + i * 128:
                                                t * 256 + (i + 1) * 128]:
                                          e.transpose(pa, ya, id_sb[:, :]),
                                          reads=["y_tok", "ident"],
                                          writes=[PK[t]],
                                          inc=(i == 1))
                                kb.op("scalar", lambda e, t=t, pa=P[t][:, 0:256],
                                      hf=hfm_h[:, t * 256:(t + 1) * 256]:
                                      e.activation(hf, pa, AF.Copy),
                                      reads=[PK[t]], writes=[f"hfh{t}"])
                                kb.op("vector", lambda e, t=t, pa=P[t][:, 0:256],
                                      hf=hfm_h[:, t * 256:(t + 1) * 256],
                                      lf=hfm_l[:, t * 256:(t + 1) * 256]:
                                      e.tensor_tensor(lf, pa, hf, OP.subtract),
                                      reads=[PK[t], f"hfh{t}"],
                                      writes=[f"hfl{t}"])
                            for ob in (0, 1):
                                hho, obx = ob // 2, ob % 2
                                for t in (t0, t1):
                                    for kdx in range(2):
                                        dsl = down_units[state["di"] + 2 * hho + kdx][1]
                                        lh = hfm_h[:, t * 256 + kdx * 128:
                                                   t * 256 + kdx * 128 + 128]
                                        ll = hfm_l[:, t * 256 + kdx * 128:
                                                   t * 256 + kdx * 128 + 128]
                                        rh = slot_ap[dsl][:, obx * 512:obx * 512 + 512]
                                        rl = slot_ap[dsl][:, 1024 + obx * 512:
                                                      1024 + obx * 512 + 512]
                                        mm3(t, lh=lh, ll=ll, rh=rh, rl=rl,
                                            reads=[f"hfh{t}", f"hfl{t}", dsl],
                                            first=(kdx == 0), last=(kdx == 1),
                                            inc_last=(kdx == 1))
                                    xa = x_cur[:, t * D + ob * 512:
                                               t * D + ob * 512 + 512]
                                    kb.op("vector", lambda e, t=t, pa=P[t][:, :],
                                          xa=xa:
                                          e.scalar_tensor_tensor(
                                              xa, pa, 0.5, xa, OP.mult, OP.add),
                                          reads=[PK[t], XC[t]],
                                          writes=[XC[t]])
                        for ob in range(2, 4):
                            hho, obx = ob // 2, ob % 2
                            for t in range(8):
                                for kdx in range(2):
                                    dsl = down_units[state["di"] + 2 * hho + kdx][1]
                                    lh = hfm_h[:, t * 256 + kdx * 128:
                                               t * 256 + kdx * 128 + 128]
                                    ll = hfm_l[:, t * 256 + kdx * 128:
                                               t * 256 + kdx * 128 + 128]
                                    rh = slot_ap[dsl][:, obx * 512:obx * 512 + 512]
                                    rl = slot_ap[dsl][:, 1024 + obx * 512:
                                                  1024 + obx * 512 + 512]
                                    mm3(t, lh=lh, ll=ll, rh=rh, rl=rl,
                                        reads=[f"hfh{t}", f"hfl{t}", dsl],
                                        first=(kdx == 0), last=(kdx == 1),
                                        inc_last=(kdx == 1))
                                xa = x_cur[:, t * D + ob * 512:
                                           t * D + ob * 512 + 512]
                                kb.op("vector", lambda e, t=t, pa=P[t][:, :],
                                      xa=xa:
                                      e.scalar_tensor_tensor(
                                          xa, pa, 0.5, xa, OP.mult, OP.add),
                                      reads=[PK[t], XC[t]],
                                      writes=[XC[t]])
                        state["di"] += 4
                for tt in range(TPS):
                    gt = s * TPS + tt
                    kb.op("scalar", lambda e, xa=x_cur[:, tt * D:(tt + 1) * D],
                          da=dst[gt * 128:(gt + 1) * 128, :]:
                          e.dma_start(out=da, in_=xa),
                          reads=[XC[tt]], writes=[f"{dstname}_{gt}"],
                          sem="xd", n=16)

        def attn_phase():
            veh3 = veh[:, :].rearrange("p (j w) -> p j w", w=132)
            vel3 = vel[:, :].rearrange("p (j w) -> p j w", w=132)
            kb.op("vector", lambda e: e.memset(veh[:, :], 0.0), writes=["veh"])
            kb.op("vector", lambda e: e.memset(veh3[:, :, 128:129], 1.0),
                  writes=["veh"])
            kb.op("vector", lambda e: e.memset(vel[:, :], 0.0), writes=["vel"])
            kv_units, vv_units = [], []
            for gt in range(NT):
                for q in range(8):
                    kv_units.append((
                        wkb[2 * q:2 * q + 2, :, :].rearrange("k p w -> p k w"),
                        SLOT[len(kv_units) % 2], 2048,
                        [f"Bwk_{q}", f"Bwk_{q}l"]))
                    vv_units.append((
                        wvb[2 * q:2 * q + 2, :, :].rearrange("k p w -> p k w"),
                        SLOT[2 + len(vv_units) % 2], 2048,
                        [f"Bwv_{q}", f"Bwv_{q}l"]))
            kneed = feeder(kv_units)
            vneed = feeder(vv_units)
            for s in range(S):
                for tt in range(TPS):
                    load_x(park1, "park1", s, tt)
                for tt in range(TPS):
                    stage_a_stats(park1, "park1", s, tt)
                for tt in range(TPS):
                    gt = s * TPS + tt
                    stage_a_norm(park1, "park1", s, tt)
                    proj(6, tt, kneed, kv_units, 8 * gt, bk_sb[:, :], "bk_sb")
                    proj(7, tt, vneed, vv_units, 8 * gt, bv_sb[:, :], "bv_sb")
                    kb.op("vector", lambda e, pa=P[6][:, :]:
                          e.tensor_scalar(scr2, pa, 0.0, None, OP.min),
                          reads=[PK[6]], writes=["scr2"])
                    kb.op("scalar", lambda e: e.activation(scr1, scr2, AF.Exp),
                          reads=["scr2"], writes=["scr1"])
                    kb.op("vector", lambda e, pa=P[6][:, :]:
                          e.scalar_tensor_tensor(scr0, pa, 0.0,
                                                 scr1, OP.max, OP.add),
                          reads=[PK[6], "scr1"], writes=["scr0"])
                    kb.op("scalar", lambda e: e.activation(
                        skb[:, 0:512], scr0, AF.Copy),
                        reads=["scr0"], writes=["skh"])
                    kb.op("vector", lambda e: e.tensor_tensor(
                        skb[:, 512:1024], scr0, skb[:, 0:512], OP.subtract),
                        reads=["scr0", "skh"], writes=["skl"])
                    kb.op("scalar", lambda e,
                          sa=P[7][:, :].rearrange("p (j w) -> p j w", w=128),
                          da=veh3[:, :, 0:128]:
                          e.activation(da, sa, AF.Copy),
                          reads=[PK[7], "veh"], writes=["vedh"])
                    kb.op("vector", lambda e,
                          sa=P[7][:, :].rearrange("p (j w) -> p j w", w=128),
                          da=vel3[:, :, 0:128], ha=veh3[:, :, 0:128]:
                          e.tensor_tensor(da, sa, ha, OP.subtract),
                          reads=[PK[7], "vedh", "vel"], writes=["vedl"])
                    for jh in range(4):
                        lh = skb[:, jh * 128:(jh + 1) * 128]
                        ll = skb[:, 512 + jh * 128:512 + (jh + 1) * 128]
                        rh = veh[:, jh * 132:jh * 132 + 132]
                        rl = vel[:, jh * 132:jh * 132 + 132]
                        kb.op("tensor", lambda e, pa=P[jh][:, 0:132], la=lh,
                              ra=rh, st=(gt == 0):
                              e.matmul(pa, la, ra, start=st, stop=False),
                              reads=["skh", "veh", "vedh"],
                              writes=[PK[jh]] if gt == 0 else (), inc=False)
                        kb.op("tensor", lambda e, pa=P[jh][:, 0:132], la=lh,
                              ra=rl: e.matmul(pa, la, ra, start=False, stop=False),
                              reads=["skh", "vel", "vedl"], inc=False)
                        kb.op("tensor", lambda e, pa=P[jh][:, 0:132], la=ll,
                              ra=rh, sp=(gt == NT - 1):
                              e.matmul(pa, la, ra, start=False, stop=sp),
                              reads=["skl", "veh", "vedh"],
                              writes=[PK[jh]] if gt == NT - 1 else (),
                              inc=(gt == NT - 1 or jh == 3))
            for jh in range(4):
                kb.op("vector", lambda e, jh=jh, pa=P[jh][:, 0:132]:
                      e.tensor_copy(kv_sb[:, jh * 132:jh * 132 + 132], pa),
                      reads=[PK[jh]], writes=["kv"])
            kb.op("gpsimd", lambda e: e.dma_start(out=cc_in_t.ap(),
                                                  in_=kv_sb[:, :]),
                  reads=["kv"], writes=["cc_in"], sem="gd", n=16)
            kb.op("gpsimd", lambda e: e.collective_compute(
                "AllReduce", OP.add,
                replica_groups=[[0, 1], [2, 3], [4, 5], [6, 7]],
                ins=[cc_in_t.ap().opt()], outs=[cc_out_t.ap().opt()]),
                reads=["cc_in"], writes=["cc_out"], sem="cc", n=1)
            kb.op("gpsimd", lambda e: e.dma_start(out=kv_sb[:, :],
                                                  in_=cc_out_t.ap()),
                  reads=["cc_out"], writes=["kv"], sem="gd", n=16)

            def kv_split():
                kb.op("scalar", lambda e: e.activation(
                    kvb[:, 0:528], kv_sb[:, :], AF.Copy),
                    reads=["kv"], writes=["kvh"])
                kb.op("vector", lambda e: e.tensor_tensor(
                    kvb[:, 528:1056], kv_sb[:, :], kvb[:, 0:528], OP.subtract),
                    reads=["kv", "kvh"], writes=["kvl"])

            q_units, o_units = [], []
            for gt in range(NT):
                for ob in range(NOB):
                    for q in range(8):
                        q_units.append((
                            wqb[ob, 2 * q:2 * q + 2, :, :].rearrange(
                                "k p w -> p k w"),
                            SLOT[len(q_units) % 4], 2048,
                            [f"Bwq_{ob}_{q}", f"Bwq_{ob}_{q}l"]))
                for k in range(KC):
                    for hh in range(2):
                        o_units.append((
                            wob[k, 2 * hh:2 * hh + 2, :, :].rearrange(
                                "o p w -> p o w"),
                            SLOT[4 + len(o_units) % 2], 2048,
                            [f"Bwo_{k}_{hh}", f"Bwo_{k}_{hh}l"]))
            qneed = feeder(q_units)
            oneed = feeder(o_units)
            qi = oi = 0
            for s in range(S):
                for tt in range(TPS):
                    load_x(park1, "park1", s, tt)
                for tt in range(TPS):
                    stage_a_stats(park1, "park1", s, tt)
                for tt in range(TPS):
                    gt = s * TPS + tt
                    stage_a_norm(park1, "park1", s, tt)
                    def p2_chain(ob):
                        qb = 6 + ob % 2
                        kb.op("vector", lambda e, pa=P[qb][:, :]:
                              e.tensor_scalar(scr2, pa, 0.0, None, OP.min),
                              reads=[PK[qb]], writes=["scr2"])
                        kb.op("scalar", lambda e:
                              e.activation(scr1, scr2, AF.Exp),
                              reads=["scr2"], writes=["scr1"])
                        kb.op("vector", lambda e, pa=P[qb][:, :]:
                              e.scalar_tensor_tensor(scr0, pa, 0.0, scr1,
                                                     OP.max, OP.add),
                              reads=[PK[qb], "scr1"], writes=["scr0"])
                        for i in range(4):
                            kb.op("tensor", lambda e, i=i,
                                  pa=P[4][:, i * 128:(i + 1) * 128]:
                                  e.transpose(pa, scr0[:, i * 128:(i + 1) * 128],
                                              id_sb[:, :]),
                                  reads=["scr0", "ident"],
                                  writes=[PK[4]] if (i == 0 or i == 3) else (),
                                  inc=(i == 3))
                        kb.op("scalar", lambda e, pa=P[4][:, :]:
                              e.activation(sqfb[:, 0:512], pa, AF.Copy),
                              reads=[PK[4]], writes=["sqh"])
                        kb.op("vector", lambda e, pa=P[4][:, :]:
                              e.tensor_tensor(sqfb[:, 512:1024], pa,
                                              sqfb[:, 0:512], OP.subtract),
                              reads=[PK[4], "sqh"], writes=["sql"])
                        for i in range(4):
                            lh = sqfb[:, i * 128:(i + 1) * 128]
                            ll = sqfb[:, 512 + i * 128:512 + (i + 1) * 128]
                            rh = kvb[:, ob * 132:ob * 132 + 132]
                            rl = kvb[:, 528 + ob * 132:528 + ob * 132 + 132]
                            kb.op("tensor", lambda e, i=i, pa=P[i][:, 0:132],
                                  la=lh, ra=rh:
                                  e.matmul(pa, la, ra, start=True, stop=False),
                                  reads=["sqh", "kvh"], writes=[PK[i]],
                                  inc=False)
                            kb.op("tensor", lambda e, i=i, pa=P[i][:, 0:132],
                                  la=lh, ra=rl:
                                  e.matmul(pa, la, ra, start=False, stop=False),
                                  reads=["sqh", "kvl"], inc=False)
                            kb.op("tensor", lambda e, i=i, pa=P[i][:, 0:132],
                                  la=ll, ra=rh:
                                  e.matmul(pa, la, ra, start=False, stop=True),
                                  reads=["sql", "kvh"], writes=[PK[i]])
                        for i in range(4):
                            kb.op("vector", lambda e, i=i, pa=P[i][:, 128:129]:
                                  e.tensor_scalar(rr[:, i:i + 1], pa, EPS_AT,
                                                  None, OP.add),
                                  reads=[PK[i]], writes=[f"r1_{i}"], strict=True)
                            kb.op("vector", lambda e, i=i:
                                  e.reciprocal(rr[:, 8 + i:9 + i],
                                               rr[:, i:i + 1]),
                                  reads=[f"r1_{i}"], writes=[f"r2_{i}"],
                                  strict=True)
                            kb.op("vector", lambda e, i=i, pa=P[i][:, 0:128]:
                                  e.tensor_scalar(scr2[:, i * 128:(i + 1) * 128],
                                                  pa, rr[:, 8 + i:9 + i], None,
                                                  OP.mult),
                                  reads=[PK[i], f"r2_{i}"], writes=["scr2"],
                                  sreads=(f"r2_{i}",))
                        for i in range(4):
                            kb.op("tensor", lambda e, i=i,
                                  pa=P[5][:, i * 128:(i + 1) * 128]:
                                  e.transpose(pa, scr2[:, i * 128:(i + 1) * 128],
                                              id_sb[:, :]),
                                  reads=["scr2", "ident"],
                                  writes=[PK[5]] if (i == 0 or i == 3) else (),
                                  inc=(i == 3))
                        kb.op("scalar", lambda e, pa=P[5][:, :],
                              da=afh[:, ob * 512:ob * 512 + 512]:
                              e.activation(da, pa, AF.Copy),
                              reads=[PK[5]], writes=[f"afh{ob}"])
                        kb.op("vector", lambda e, pa=P[5][:, :],
                              da=afl[:, ob * 512:ob * 512 + 512],
                              ha=afh[:, ob * 512:ob * 512 + 512]:
                              e.tensor_tensor(da, pa, ha, OP.subtract),
                              reads=[PK[5], f"afh{ob}"], writes=[f"afl{ob}"])

                    for ob in range(NOB):
                        proj(6 + ob % 2, tt, qneed, q_units, qi,
                             bq_sb[:, ob * 512:ob * 512 + 512], "bq_sb")
                        qi += 8
                        if s == 0 and tt == 0 and ob == 0:
                            kv_split()
                        if ob >= 1:
                            p2_chain(ob - 1)
                    p2_chain(3)
                    AFR = ["afh0", "afh1", "afh2", "afh3",
                           "afl0", "afl1", "afl2", "afl3"]
                    for k in range(KC):
                        for hh in range(2):
                            oneed(oi)
                            osl = o_units[oi][1]
                            for obx in range(2):
                                ob = 2 * hh + obx
                                bank = 4 + ob
                                lh = afh[:, k * 128:(k + 1) * 128]
                                ll = afl[:, k * 128:(k + 1) * 128]
                                rh = slot_ap[osl][:, obx * 1024:obx * 1024 + 512]
                                rl = slot_ap[osl][:, obx * 1024 + 512:obx * 1024 + 1024]
                                mm3(bank, lh=lh, ll=ll, rh=rh, rl=rl,
                                    reads=AFR + [osl],
                                    first=(k == 0), last=(k == KC - 1),
                                    inc_last=(k == KC - 1 or obx == 1))
                            oi += 1
                    for ob in range(NOB):
                        bank = 4 + ob
                        xa = x_cur[:, tt * D + ob * 512:tt * D + ob * 512 + 512]
                        kb.op("vector", lambda e, pa=P[bank][:, :], xa=xa:
                              e.tensor_tensor(xa, xa, pa, OP.add),
                              reads=[PK[bank], XC[tt]], writes=[XC[tt]])
                    kb.op("scalar", lambda e, xa=x_cur[:, tt * D:(tt + 1) * D],
                          da=park2[gt * 128:(gt + 1) * 128, :]:
                          e.dma_start(out=da, in_=xa),
                          reads=[XC[tt]], writes=[f"park2_{gt}"],
                          sem="xd", n=16)

        ffn_phase(x_in, None, park1)
        attn_phase()
        ffn_phase(park2, "park2", outd)

        kb._wait("scalar", "xd", kb.cnt["xd"])
        kb._wait("gpsimd", "gd", kb.cnt["gd"])
        kb._wait("sync", "wd", kb.cnt["wd"])

        with nc.Block() as block:
            @block.sync
            def _(e):
                for f in kb.ops["sync"]:
                    f(e)

            @block.scalar
            def _(e):
                for f in kb.ops["scalar"]:
                    f(e)

            @block.vector
            def _(e):
                for f in kb.ops["vector"]:
                    f(e)

            @block.tensor
            def _(e):
                for f in kb.ops["tensor"]:
                    f(e)

            @block.gpsimd
            def _(e):
                for f in kb.ops["gpsimd"]:
                    f(e)

    return nc


def _host_prep(inputs):
    P512 = np.empty(512, np.int64)
    g = np.arange(64)
    for h in range(2):
        for i in range(4):
            P512[h * 256 + i * 64:h * 256 + (i + 1) * 64] = 4 * (64 * h + g) + i

    w_gate = np.asarray(inputs["L0_ffn_w_gate"], np.float32)
    w_up = np.asarray(inputs["L0_ffn_w_up"], np.float32)
    w_down = np.asarray(inputs["L0_ffn_w_down"], np.float32)
    Wq = np.asarray(inputs["L0_attn_Wq"], np.float32)
    Wk = np.asarray(inputs["L0_attn_Wk"], np.float32)
    Wv = np.asarray(inputs["L0_attn_Wv"], np.float32)
    Wo = np.asarray(inputs["L0_attn_Wo"], np.float32)

    wg_h = np.ascontiguousarray(
        w_gate.reshape(D, NJ, JW)[:, :, P512].transpose(1, 0, 2)
    ).reshape(NJ, KC, 128, JW)
    wu_h = np.ascontiguousarray(
        w_up.reshape(D, NJ, JW)[:, :, P512].transpose(1, 0, 2)
    ).reshape(NJ, KC, 128, JW)
    wdn_h = np.ascontiguousarray(
        w_down.reshape(NJ, JW, D)[:, P512, :]).reshape(NJ, 4, 128, D)
    wq_h = np.ascontiguousarray(
        Wq.T.reshape(KC, 128, NOB, JW).transpose(2, 0, 1, 3))
    wo_h = np.ascontiguousarray(Wo.T.reshape(KC, 128, NOB, JW).transpose(0, 2, 1, 3))
    wk_h = np.ascontiguousarray(Wk.T.reshape(KC, 128, JW))
    wv_h = np.ascontiguousarray(Wv.T.reshape(KC, 128, JW))

    base = {
        "wg": wg_h, "wu": wu_h, "wdn": wdn_h, "wq": wq_h, "wo": wo_h,
        "wk": wk_h, "wv": wv_h,
        "bq": np.ascontiguousarray(
            np.asarray(inputs["L0_attn_bq"], np.float32).reshape(1, D)),
        "bk": np.ascontiguousarray(
            np.asarray(inputs["L0_attn_bk"], np.float32).reshape(1, JW)),
        "bv": np.ascontiguousarray(
            np.asarray(inputs["L0_attn_bv"], np.float32).reshape(1, JW)),
        "ident": np.eye(128, dtype=np.float32),
    }
    x = np.asarray(inputs["L0_attn_hidden_states"], np.float32)
    in_maps = []
    for c in range(8):
        m = dict(base)
        m["x_in"] = np.ascontiguousarray(
            x[c // 2, (c % 2) * T:(c % 2 + 1) * T, :])
        in_maps.append(m)
    return in_maps


def kernel(**inputs):
    in_maps = _host_prep(inputs)
    nc = build_program()
    res = run_bass_kernel_spmd(nc, in_maps, core_ids=list(range(8)))
    outp = np.empty((4, 4096, D), np.float32)
    for c in range(8):
        outp[c // 2, (c % 2) * T:(c % 2 + 1) * T, :] = res.results[c]["out"]
    return outp
